# version 02 (final)
# baseline (speedup 1.0000x reference)
import numpy as np
from contextlib import ExitStack
import concourse.bass as bass
import concourse.mybir as mybir
from concourse.bass_utils import run_bass_kernel_spmd

F32 = mybir.dt.float32
BF16 = mybir.dt.bfloat16
AF = mybir.ActivationFunctionType
ALU = mybir.AluOpType
AX = mybir.AxisListType

NCORES = 8
S = 2048
D = 1024
LSEQ = 8192


def A(x):
    if x is None or isinstance(x, (int, float, bass.AP)):
        return x
    return x[:, :]


class Prog:
    def __init__(self):
        self.nc = bass.Bass("TRN2", target_bir_lowering=False)
        self.ops = []
        self.st = ExitStack()

    def sb(self, name, shape, dt=F32):
        return self.st.enter_context(self.nc.sbuf_tensor(name, shape, dt))

    def ps(self, name, shape, dt=F32):
        return self.st.enter_context(self.nc.psum_tensor(name, shape, dt))

    def din(self, name, shape):
        return self.nc.dram_tensor(name, shape, F32, kind="ExternalInput").ap()

    def dout(self, name, shape):
        return self.nc.dram_tensor(name, shape, F32, kind="ExternalOutput").ap()

    def dma(self, out, in_):
        self.ops.append(("sync", lambda e, o=A(out), i=A(in_): e.dma_start(out=o, in_=i), 16))

    def t(self, fn):
        self.ops.append(("tensor", fn, 1))

    def v(self, fn):
        self.ops.append(("vector", fn, 1))

    def sc(self, fn):
        self.ops.append(("scalar", fn, 1))

    def mm(self, out, lhsT, rhs, start, stop):
        self.t(lambda e, o=A(out), l=A(lhsT), r=A(rhs), s1=start, s2=stop:
               e.matmul(out=o, lhsT=l, rhs=r, start=s1, stop=s2))

    def trans(self, out_sb, in_sb, psT, ident, p=128, f=128):
        self.t(lambda e, o=psT, i=A(in_sb), idn=ident, pp=p, ff=f:
               e.transpose(out=o[0:ff, 0:pp], in_=i, identity=idn[0:pp, 0:pp]))
        self.v(lambda e, o=A(out_sb), i=psT, pp=p, ff=f:
               e.tensor_copy(out=o, in_=i[0:ff, 0:pp]))

    def cast(self, out, in_):
        self.v(lambda e, o=A(out), i=A(in_): e.tensor_copy(out=o, in_=i))

    def act(self, out, in_, func, bias=0.0, scale=1.0):
        self.sc(lambda e, o=A(out), i=A(in_), f=func, b=A(bias), s=scale:
                e.activation(out=o, in_=i, func=f, bias=b, scale=s))

    def smul(self, out, in_, c):
        self.sc(lambda e, o=A(out), i=A(in_), m=c: e.mul(out=o, in_=i, mul=m))

    def tt(self, out, a, b, op):
        self.v(lambda e, o=A(out), x=A(a), y=A(b), p=op:
               e.tensor_tensor(out=o, in0=x, in1=y, op=p))

    def tss(self, out, in_, scalar, op):
        self.v(lambda e, o=A(out), i=A(in_), s=A(scalar), p=op:
               e.tensor_single_scalar(out=o, in_=i, scalar=s, op=p))

    def ts2(self, out, in_, s1, s2, op0, op1):
        if s2 is None:
            self.v(lambda e, o=A(out), i=A(in_), a=A(s1), p0=op0:
                   e.tensor_single_scalar(out=o, in_=i, scalar=a, op=p0))
        else:
            self.v(lambda e, o=A(out), i=A(in_), a=A(s1), b=A(s2), p0=op0, p1=op1:
                   e.tensor_scalar(out=o, in0=i, scalar1=a, scalar2=b, op0=p0, op1=p1))

    def stt(self, out, in0, scalar, in1, op0, op1):
        self.v(lambda e, o=A(out), i0=A(in0), s=A(scalar), i1=A(in1), p0=op0, p1=op1:
               e.scalar_tensor_tensor(out=o, in0=i0, scalar=s, in1=i1, op0=p0, op1=p1))

    def rsum(self, out, in_):
        self.v(lambda e, o=A(out), i=A(in_): e.reduce_sum(out=o, in_=i, axis=AX.X))

    def rmax(self, out, in_):
        self.v(lambda e, o=A(out), i=A(in_): e.reduce_max(out=o, in_=i, axis=AX.X))

    def recip(self, out, in_):
        self.v(lambda e, o=A(out), i=A(in_): e.reciprocal(out=o, in_=i))

    def memset(self, ap, c):
        self.v(lambda e, a=A(ap), cc=c: e.memset(ap=a, constant=cc))

    def layernorm(self, out, x, scratch, stats, eps):
        mu = stats[:, 0:1]; m2 = stats[:, 1:2]; var = stats[:, 2:3]
        sd = stats[:, 3:4]; rstd = stats[:, 4:5]; musq = stats[:, 5:6]
        self.rsum(mu, x)
        self.smul(mu, mu, 1.0 / D)
        self.act(scratch, x, AF.Square)
        self.rsum(m2, scratch)
        self.smul(m2, m2, 1.0 / D)
        self.act(musq, mu, AF.Square)
        self.tt(var, m2, musq, ALU.subtract)
        self.tss(var, var, eps, ALU.add)
        self.act(sd, var, AF.Sqrt)
        self.recip(rstd, sd)
        self.ts2(out, x, mu, rstd, ALU.subtract, ALU.mult)

    def softmax_unnorm(self, out_bf, s_sb, stats, scale):
        rm = stats[:, 0:1]; nm = stats[:, 1:2]; rs = stats[:, 2:3]; rinv = stats[:, 4:5]
        self.rmax(rm, s_sb)
        self.smul(nm, rm, -scale)
        self.act(out_bf, s_sb, AF.Exp, bias=nm, scale=scale)
        self.rsum(rs, out_bf)
        self.recip(rinv, rs)

    def fin(self):
        nc = self.nc
        ops = self.ops
        cums = []
        c = 0
        for (_, _, inc) in ops:
            cums.append(c)
            c += inc
        total = c
        sem = self.st.enter_context(nc.semaphore("chain"))
        blk = self.st.enter_context(nc.Block())

        def mk(en):
            def body(eng):
                for i, (e, fn, inc) in enumerate(ops):
                    if e != en:
                        continue
                    if cums[i] > 0:
                        eng.wait_ge(sem, cums[i])
                    fn(eng).then_inc(sem, inc)
                if en == "sync":
                    eng.wait_ge(sem, total)
            return body

        blk.sync(mk("sync"))
        blk.tensor(mk("tensor"))
        blk.vector(mk("vector"))
        blk.scalar(mk("scalar"))
        self.st.close()
        return nc


def fm_tiles(p, xfm, xb, psT, ident, t):
    for dt in range(8):
        p.trans(xfm[dt][:, t * 128:(t + 1) * 128], xb[:, dt * 128:(dt + 1) * 128], psT, ident)


def build_p1():
    p = Prog()
    x = p.din("x", [S, D])
    Wg = p.din("Wg", [D, 3])
    bg_bc = p.din("bg_bc", [128, 3])
    Ws = p.din("Ws", [D, 1])
    bs_bc = p.din("bs_bc", [128, 1])
    Wc = p.din("Wc", [4 * D, D])
    bc_bc = p.din("bc_bc", [128, D])
    identI = p.din("ident", [128, 128])
    xn_o = p.dout("xn", [S, D])
    gates_o = p.dout("gates", [S, 4])
    sc_o = p.dout("scores", [S, 1])
    comp_o = p.dout("comp", [512, D])

    ident = p.sb("identb", [128, 128], BF16)
    idf = p.sb("idf", [128, 128], F32)
    p.dma(idf[:, :], identI[:, :])
    p.cast(ident, idf)

    xt = p.sb("xt", [128, D])
    scr = p.sb("scr", [128, D])
    stats = p.sb("stats", [128, 8])
    xnb = p.sb("xnb", [128, D], BF16)
    xnt = p.sb("xnt", [128, D])
    lo32 = p.sb("lo32", [128, D])
    lob = p.sb("lob", [128, D], BF16)
    xfm = [p.sb(f"xfm{i}", [128, S], BF16) for i in range(8)]
    xfmlo = [p.sb(f"xfmlo{i}", [128, S], BF16) for i in range(8)]
    psT = p.ps("psT", [128, 512], BF16)
    psA = p.ps("psA", [128, 512])
    psB = p.ps("psB", [128, 512])
    wgb = p.sb("wgb", [128, 24], BF16)
    wgf = p.sb("wgf", [128, 3])
    wsb = p.sb("wsb", [128, 8], BF16)
    wsf = p.sb("wsf", [128, 8])
    wslo = p.sb("wslo", [128, 8], BF16)
    gt = p.sb("gt", [128, 4])
    bgs = p.sb("bgs", [128, 3])
    p.dma(bgs[:, :], bg_bc[:, :])
    bss = p.sb("bss", [128, 1])
    p.dma(bss[:, :], bs_bc[:, :])

    for dt in range(8):
        p.dma(wgf[:, :], Wg[dt * 128:(dt + 1) * 128, :])
        p.cast(wgb[:, dt * 3:(dt + 1) * 3], wgf)
        p.dma(wsf[:, dt:dt + 1], Ws[dt * 128:(dt + 1) * 128, :])
    p.cast(wsb, wsf)
    wtmp = p.sb("wtmp", [128, 8])
    p.cast(wtmp, wsb)
    p.tt(wtmp, wsf, wtmp, ALU.subtract)
    p.cast(wslo, wtmp)

    for t in range(16):
        p.dma(xt[:, :], x[t * 128:(t + 1) * 128, :])
        p.layernorm(xnt, xt, scr, stats, 1e-5)
        p.dma(xn_o[t * 128:(t + 1) * 128, :], xnt[:, :])
        p.cast(xnb, xnt)
        fm_tiles(p, xfm, xnb, psT, ident, t)
        p.cast(lo32, xnb)
        p.tt(lo32, xnt, lo32, ALU.subtract)
        p.cast(lob, lo32)
        fm_tiles(p, xfmlo, lob, psT, ident, t)
        for dt in range(8):
            p.mm(psA[:, 0:3], xfm[dt][:, t * 128:(t + 1) * 128], wgb[:, dt * 3:(dt + 1) * 3],
                 dt == 0, dt == 7)
        p.tt(gt[:, 0:3], psA[:, 0:3], bgs, ALU.add)
        p.act(gt[:, 0:3], gt[:, 0:3], AF.Sigmoid)
        p.rsum(stats[:, 6:7], gt[:, 0:3])
        p.tss(stats[:, 6:7], stats[:, 6:7], 1e-6, ALU.add)
        p.recip(stats[:, 7:8], stats[:, 6:7])
        p.ts2(gt[:, 0:3], gt[:, 0:3], stats[:, 7:8], None, ALU.mult, ALU.mult)
        p.memset(gt[:, 3:4], 0.0)
        p.dma(gates_o[t * 128:(t + 1) * 128, :], gt[:, :])
        for dt in range(8):
            p.mm(psB[:, 0:1], xfm[dt][:, t * 128:(t + 1) * 128], wsb[:, dt:dt + 1],
                 dt == 0, False)
        for dt in range(8):
            p.mm(psB[:, 0:1], xfmlo[dt][:, t * 128:(t + 1) * 128], wsb[:, dt:dt + 1],
                 False, False)
        for dt in range(8):
            p.mm(psB[:, 0:1], xfm[dt][:, t * 128:(t + 1) * 128], wslo[:, dt:dt + 1],
                 False, dt == 7)
        p.ts2(scr[:, 0:1], psB[:, 0:1], bss[:, 0:1], None, ALU.add, ALU.add)
        p.dma(sc_o[t * 128:(t + 1) * 128, :], scr[:, 0:1])

    blocksb = p.sb("blocksb", [128, 4096], BF16)
    bstage = p.sb("bstage", [128, 4096])
    bfms = [p.sb(f"bfms{i}", [128, 4096], BF16) for i in range(4)]
    wcball = p.sb("wcball", [128, 32 * 512], BF16)
    wcf = p.sb("wcf", [128, 512])
    bcs = p.sb("bcs", [128, D])
    p.dma(bcs[:, :], bc_bc[:, :])
    compt = [p.sb(f"compt{i}", [128, D]) for i in range(4)]
    for ct in range(4):
        p.dma(bstage[:, :], xn_o[ct * 512:(ct + 1) * 512, :].rearrange(
            "(p j) d -> p (j d)", p=128))
        p.cast(blocksb, bstage)
        for dt in range(32):
            p.trans(bfms[ct][:, dt * 128:(dt + 1) * 128],
                    blocksb[:, dt * 128:(dt + 1) * 128], psT, ident)
    for ch in range(2):
        for dt in range(32):
            p.dma(wcf[:, :], Wc[dt * 128:(dt + 1) * 128, ch * 512:(ch + 1) * 512])
            p.cast(wcball[:, dt * 512:(dt + 1) * 512], wcf)
        for ct in range(4):
            for dt in range(32):
                p.mm(psA[:, :], bfms[ct][:, dt * 128:(dt + 1) * 128],
                     wcball[:, dt * 512:(dt + 1) * 512], dt == 0, dt == 31)
            p.tt(compt[ct][:, ch * 512:(ch + 1) * 512], psA[:, :],
                 bcs[:, ch * 512:(ch + 1) * 512], ALU.add)
    for ct in range(4):
        p.dma(comp_o[ct * 128:(ct + 1) * 128, :], compt[ct][:, :])
    return p


def build_p2():
    p = Prog()
    xn = p.din("xn", [S, D])
    gwT = p.din("gwT", [D, 16])
    gb_bc = p.din("gb_bc", [128, 16])
    w1T = p.din("w1T", [17, D, 512])
    w3T = p.din("w3T", [17, D, 512])
    w2T = p.din("w2T", [17, 512, D])
    identI = p.din("ident", [128, 128])
    moe_o = p.dout("moe", [S, D])

    ident = p.sb("identb", [128, 128], BF16)
    idf = p.sb("idf", [128, 128], F32)
    p.dma(idf[:, :], identI[:, :])
    p.cast(ident, idf)
    psT = p.ps("psT", [128, 512], BF16)
    psA = p.ps("psA", [128, 512])
    psB = p.ps("psB", [128, 512])
    psC = p.ps("psC", [128, 512])

    xt = p.sb("xt", [128, D])
    xnb = p.sb("xnb", [128, D], BF16)
    xfm = [p.sb(f"xfm{i}", [128, S], BF16) for i in range(8)]
    gwb = p.sb("gwb", [128, 128], BF16)
    gwf = p.sb("gwf", [128, 16])
    gbs = p.sb("gbs", [128, 16])
    p.dma(gbs[:, :], gb_bc[:, :])
    combsb = p.sb("combsb", [128, 256])
    sc16 = p.sb("sc16", [128, 16])
    s16 = p.sb("s16", [128, 16])
    gmax = p.sb("gmax", [128, 4])
    cntg = p.sb("cntg", [128, 4])
    keep = p.sb("keep", [128, 4])
    pen = p.sb("pen", [128, 4])
    gtt = p.sb("gtt", [128, 16])
    cnte = p.sb("cnte", [128, 16])
    sel = p.sb("sel", [128, 16])

    for dt in range(8):
        p.dma(gwf[:, :], gwT[dt * 128:(dt + 1) * 128, :])
        p.cast(gwb[:, dt * 16:(dt + 1) * 16], gwf)

    yacc = p.sb("yacc", [128, 16384])

    for t in range(16):
        p.dma(xt[:, :], xn[t * 128:(t + 1) * 128, :])
        p.cast(xnb, xt)
        fm_tiles(p, xfm, xnb, psT, ident, t)
        for dt in range(8):
            p.mm(psA[:, 0:16], xfm[dt][:, t * 128:(t + 1) * 128], gwb[:, dt * 16:(dt + 1) * 16],
                 dt == 0, dt == 7)
        p.act(sc16, psA[:, 0:16], AF.Sigmoid)
        p.tt(s16, sc16, gbs, ALU.add)
        p.v(lambda e, o=A(gmax), i=A(s16): e.reduce_max(
            out=o, in_=i.rearrange("p (g f) -> p g f", g=4), axis=AX.X))
        for g in range(4):
            p.ts2(cntg, gmax, gmax[:, g:g + 1], None, ALU.is_gt, ALU.is_gt)
            p.rsum(keep[:, g:g + 1], cntg)
        p.tss(keep, keep, 2.0, ALU.is_lt)
        p.ts2(pen, keep, 1.0, 1e30, ALU.subtract, ALU.mult)
        for g in range(4):
            p.ts2(s16[:, g * 4:(g + 1) * 4], s16[:, g * 4:(g + 1) * 4], pen[:, g:g + 1],
                  None, ALU.add, ALU.add)
        for ei in range(16):
            p.ts2(gtt, s16, s16[:, ei:ei + 1], None, ALU.is_gt, ALU.is_gt)
            p.rsum(cnte[:, ei:ei + 1], gtt)
        p.tss(sel, cnte, 4.0, ALU.is_lt)
        p.tt(combsb[:, t * 16:(t + 1) * 16], sc16, sel, ALU.mult)

    onecol = p.sb("onecol", [128, 1])
    p.memset(onecol, 1.0)
    w1b = p.sb("w1b", [128, 4096], BF16)
    w3b = p.sb("w3b", [128, 4096], BF16)
    w2b = p.sb("w2b", [128, 4096], BF16)
    wst = p.sb("wst", [128, 1024])
    hs = p.sb("hs", [128, 512])
    hb = p.sb("hb", [128, 512], BF16)
    hfm = p.sb("hfm", [128, 512], BF16)

    for e in range(17):
        for dt in range(8):
            p.dma(wst[:, 0:512], w1T[e, dt * 128:(dt + 1) * 128, :])
            p.cast(w1b[:, dt * 512:(dt + 1) * 512], wst[:, 0:512])
            p.dma(wst[:, 0:512], w3T[e, dt * 128:(dt + 1) * 128, :])
            p.cast(w3b[:, dt * 512:(dt + 1) * 512], wst[:, 0:512])
        for it in range(4):
            p.dma(wst[:, :], w2T[e, it * 128:(it + 1) * 128, :])
            p.cast(w2b[:, it * 1024:(it + 1) * 1024], wst[:, :])
        for t in range(16):
            for dt in range(8):
                p.mm(psA[:, :], xfm[dt][:, t * 128:(t + 1) * 128],
                     w1b[:, dt * 512:(dt + 1) * 512], dt == 0, dt == 7)
            for dt in range(8):
                p.mm(psB[:, :], xfm[dt][:, t * 128:(t + 1) * 128],
                     w3b[:, dt * 512:(dt + 1) * 512], dt == 0, dt == 7)
            p.act(hs, psA[:, :], AF.Silu)
            p.tt(hb, hs, psB[:, :], ALU.mult)
            for it in range(4):
                p.trans(hfm[:, it * 128:(it + 1) * 128], hb[:, it * 128:(it + 1) * 128],
                        psT, ident)
            col = combsb[:, t * 16 + e:t * 16 + e + 1] if e < 16 else onecol[:, 0:1]
            for dh in range(2):
                for it in range(4):
                    p.mm(psC[:, :], hfm[:, it * 128:(it + 1) * 128],
                         w2b[:, it * 1024 + dh * 512:it * 1024 + dh * 512 + 512],
                         it == 0, it == 3)
                dst = yacc[:, t * 1024 + dh * 512: t * 1024 + dh * 512 + 512]
                if e == 0:
                    p.ts2(dst, psC[:, :], col, None, ALU.mult, ALU.mult)
                else:
                    p.stt(dst, psC[:, :], col, dst, ALU.mult, ALU.add)
    for t in range(16):
        p.dma(moe_o[t * 128:(t + 1) * 128, :], yacc[:, t * 1024:(t + 1) * 1024])
    return p


def build_p3k():
    p = Prog()
    comp = p.din("comp", [512, D])
    Wq = p.din("Wq", [D, D])
    bqc = p.din("bqc", [D, 1])
    Wk = p.din("Wk", [D, D])
    bkc = p.din("bkc", [D, 1])
    Wv = p.din("Wv", [D, D])
    bv_bc = p.din("bv_bc", [128, D])
    identI = p.din("ident", [128, 128])
    qfm_o = p.dout("qfm", [D, 512])
    kfm_o = p.dout("kfm", [D, 512])
    vtok_o = p.dout("vtok", [512, D])

    ident = p.sb("identb", [128, 128], BF16)
    idf = p.sb("idf", [128, 128], F32)
    p.dma(idf[:, :], identI[:, :])
    p.cast(ident, idf)
    psT = p.ps("psT", [128, 512], BF16)
    psA = p.ps("psA", [128, 512])

    cst = p.sb("cst", [128, D])
    cb = p.sb("cb", [128, D], BF16)
    cfm = [p.sb(f"cfm{i}", [128, 512], BF16) for i in range(8)]
    for t in range(4):
        p.dma(cst[:, :], comp[t * 128:(t + 1) * 128, :])
        p.cast(cb, cst)
        fm_tiles(p, cfm, cb, psT, ident, t)

    wqb = [p.sb(f"wqb{i}", [128, D], BF16) for i in range(8)]
    wkb = [p.sb(f"wkb{i}", [128, D], BF16) for i in range(8)]
    wvb = [p.sb(f"wvb{i}", [128, D], BF16) for i in range(8)]
    wstage = p.sb("wstage", [128, D])
    for dt in range(8):
        p.dma(wstage[:, :], Wq[dt * 128:(dt + 1) * 128, :])
        p.cast(wqb[dt], wstage)
        p.dma(wstage[:, :], Wk[dt * 128:(dt + 1) * 128, :])
        p.cast(wkb[dt], wstage)
        p.dma(wstage[:, :], Wv[dt * 128:(dt + 1) * 128, :])
        p.cast(wvb[dt], wstage)
    bqs = p.sb("bqs", [128, 8])
    bks = p.sb("bks", [128, 8])
    for dt in range(8):
        p.dma(bqs[:, dt:dt + 1], bqc[dt * 128:(dt + 1) * 128, :])
        p.dma(bks[:, dt:dt + 1], bkc[dt * 128:(dt + 1) * 128, :])
    bvs = p.sb("bvs", [128, D])
    p.dma(bvs[:, :], bv_bc[:, :])
    osb = p.sb("osb", [128, 512])
    for ot in range(8):
        for dt in range(8):
            p.mm(psA[:, :], wqb[dt][:, ot * 128:(ot + 1) * 128], cfm[dt][:, :], dt == 0, dt == 7)
        p.ts2(osb, psA[:, :], bqs[:, ot:ot + 1], None, ALU.add, ALU.add)
        p.dma(qfm_o[ot * 128:(ot + 1) * 128, :], osb[:, :])
        for dt in range(8):
            p.mm(psA[:, :], wkb[dt][:, ot * 128:(ot + 1) * 128], cfm[dt][:, :], dt == 0, dt == 7)
        p.ts2(osb, psA[:, :], bks[:, ot:ot + 1], None, ALU.add, ALU.add)
        p.dma(kfm_o[ot * 128:(ot + 1) * 128, :], osb[:, :])
    for t in range(4):
        for dh in range(2):
            for dt in range(8):
                p.mm(psA[:, :], cfm[dt][:, t * 128:(t + 1) * 128],
                     wvb[dt][:, dh * 512:(dh + 1) * 512], dt == 0, dt == 7)
            p.tt(osb, psA[:, :], bvs[:, dh * 512:(dh + 1) * 512], ALU.add)
            p.dma(vtok_o[t * 128:(t + 1) * 128, dh * 512:(dh + 1) * 512], osb[:, :])
    return p


def build_p3a():
    p = Prog()
    qfm = p.din("qfm", [D, 512])
    kfm = p.din("kfm", [D, 2048])
    vtok = p.din("vtok", [2048, D])
    identI = p.din("ident", [128, 128])
    compout_o = p.dout("compout", [512, D])

    ident = p.sb("identb", [128, 128], BF16)
    idf = p.sb("idf", [128, 128], F32)
    p.dma(idf[:, :], identI[:, :])
    p.cast(ident, idf)
    psT = p.ps("psT", [128, 512], BF16)
    psA = p.ps("psA", [128, 512])
    psC = p.ps("psC", [128, 512])

    stage = p.sb("stage", [128, 2048])
    qb = [p.sb(f"qb{i}", [128, 512], BF16) for i in range(8)]
    kb = [p.sb(f"kb{i}", [128, 2048], BF16) for i in range(8)]
    vb = [p.sb(f"vb{i}", [128, D], BF16) for i in range(16)]
    for dt in range(8):
        p.dma(stage[:, 0:512], qfm[dt * 128:(dt + 1) * 128, :])
        p.cast(qb[dt], stage[:, 0:512])
        p.dma(stage[:, :], kfm[dt * 128:(dt + 1) * 128, :])
        p.cast(kb[dt], stage)
    for t in range(16):
        p.dma(stage[:, 0:D], vtok[t * 128:(t + 1) * 128, :])
        p.cast(vb[t], stage[:, 0:D])
    Ssb = p.sb("Ssb", [128, 2048])
    Pb = p.sb("Pb", [128, 2048], BF16)
    PTb = p.sb("PTb", [128, 2048], BF16)
    osb = p.sb("osb", [128, D])
    stats = p.sb("stats", [128, 8])
    for qt in range(4):
        for ch in range(4):
            for dt in range(8):
                p.mm(psA[:, :], qb[dt][:, qt * 128:(qt + 1) * 128],
                     kb[dt][:, ch * 512:(ch + 1) * 512], dt == 0, dt == 7)
            p.cast(Ssb[:, ch * 512:(ch + 1) * 512], psA[:, :])
        p.softmax_unnorm(Pb, Ssb, stats, 0.125)
        for kt in range(16):
            p.trans(PTb[:, kt * 128:(kt + 1) * 128], Pb[:, kt * 128:(kt + 1) * 128], psT, ident)
        for dh in range(2):
            for kt in range(16):
                p.mm(psC[:, :], PTb[:, kt * 128:(kt + 1) * 128],
                     vb[kt][:, dh * 512:(dh + 1) * 512], kt == 0, kt == 15)
            p.ts2(osb[:, dh * 512:(dh + 1) * 512], psC[:, :], stats[:, 4:5], None,
                  ALU.mult, ALU.mult)
        p.dma(compout_o[qt * 128:(qt + 1) * 128, :], osb[:, :])
    return p


def make_p3b(win_base):
    def build():
        p = Prog()
        xnfull = p.din("xnfull", [LSEQ, D])
        scores = p.din("scores", [LSEQ, 1])
        Wq = p.din("Wq", [D, D])
        bqc = p.din("bqc", [D, 1])
        Wk = p.din("Wk", [D, D])
        bkc = p.din("bkc", [D, 1])
        Wv = p.din("Wv", [D, D])
        bv_bc = p.din("bv_bc", [128, D])
        identI = p.din("ident", [128, 128])
        iota_bc = p.din("iota_bc", [128, 512])
        ltriI = p.din("ltri", [128, 128])
        selout_o = p.dout("selout", [512, D])
        winout_o = p.dout("winout", [2048, D])
        posscr = p.nc.dram_tensor("posscr", [LSEQ, 1], F32, kind="Internal").ap()

        ident = p.sb("identb", [128, 128], BF16)
        idf = p.sb("idf", [128, 128], F32)
        p.dma(idf[:, :], identI[:, :])
        p.cast(ident, idf)
        iob = p.sb("iob", [128, 512])
        p.dma(iob[:, :], iota_bc[:, :])
        psT = p.ps("psT", [128, 512], BF16)
        psA = p.ps("psA", [128, 512])
        psB = p.ps("psB", [128, 512])

        wqb = [p.sb(f"wqb{i}", [128, D], BF16) for i in range(8)]
        wkb = [p.sb(f"wkb{i}", [128, D], BF16) for i in range(8)]
        wvb = [p.sb(f"wvb{i}", [128, D], BF16) for i in range(8)]
        wstage = p.sb("wstage", [128, D])
        for dt in range(8):
            p.dma(wstage[:, :], Wq[dt * 128:(dt + 1) * 128, :])
            p.cast(wqb[dt], wstage)
            p.dma(wstage[:, :], Wk[dt * 128:(dt + 1) * 128, :])
            p.cast(wkb[dt], wstage)
            p.dma(wstage[:, :], Wv[dt * 128:(dt + 1) * 128, :])
            p.cast(wvb[dt], wstage)
        bqs = p.sb("bqs", [128, 8])
        bks = p.sb("bks", [128, 8])
        for dt in range(8):
            p.dma(bqs[:, dt:dt + 1], bqc[dt * 128:(dt + 1) * 128, :])
            p.dma(bks[:, dt:dt + 1], bkc[dt * 128:(dt + 1) * 128, :])
        bvs = p.sb("bvs", [128, D])
        p.dma(bvs[:, :], bv_bc[:, :])
        stats = p.sb("stats", [128, 8])

        sT = p.sb("sT", [128, 64])
        p.dma(sT[:, :], scores[:, :].rearrange("(p c) 1 -> p c", p=128))
        sbc = p.sb("sbc", [128, LSEQ])
        for pp in range(128):
            p.dma(sbc[pp:pp + 1, :], scores[:, :].rearrange("(a b) 1 -> a b", a=1))
        gtb = p.sb("gtb", [128, LSEQ], BF16)
        cnt = p.sb("cnt", [128, 64])
        msel = p.sb("msel", [128, 64])
        pos = p.sb("pos", [128, 64])
        run = p.sb("run", [128, 1])
        possel = p.sb("possel", [128, 64])
        for c in range(64):
            p.ts2(gtb, sbc, sT[:, c:c + 1], None, ALU.is_gt, ALU.is_gt)
            p.rsum(cnt[:, c:c + 1], gtb)
        p.tss(msel, cnt, 512.0, ALU.is_lt)
        p.memset(run, 0.0)
        for c in range(64):
            p.cast(pos[:, c:c + 1], run)
            p.tt(run, run, msel[:, c:c + 1], ALU.add)
        ltf = p.sb("ltf", [128, 128])
        ltb = p.sb("ltb", [128, 128], BF16)
        p.dma(ltf[:, :], ltriI[:, :])
        p.cast(ltb, ltf)
        runb = p.sb("runb", [128, 1], BF16)
        p.cast(runb, run)
        p.mm(psA[:, 0:1], ltb, runb, True, True)
        cumex = p.sb("cumex", [128, 1])
        p.cast(cumex, psA[0:128, 0:1])
        p.ts2(pos, pos, cumex[:, 0:1], None, ALU.add, ALU.add)
        p.ts2(possel, pos, 1.0, None, ALU.add, ALU.add)
        p.tt(possel, possel, msel, ALU.mult)
        p.tss(possel, possel, 1.0, ALU.subtract)
        p.dma(posscr[:, :].rearrange("(p c) 1 -> p c", p=128), possel[:, :])
        pcol = p.sb("pcol", [128, 64])
        for kc in range(64):
            p.dma(pcol[:, kc:kc + 1], posscr[kc * 128:(kc + 1) * 128, :])
        selx = [p.sb(f"selx{i}", [128, D], BF16) for i in range(4)]
        PTg = p.sb("PTg", [128, 128], BF16)
        xstage = p.sb("xstage", [128, D])
        xstb = p.sb("xstb", [128, D], BF16)
        for ss in range(4):
            for dh in range(2):
                for it in range(64):
                    p.dma(xstage[:, :], xnfull[it * 128:(it + 1) * 128, :])
                    p.cast(xstb, xstage)
                    p.ts2(PTg, iob[:, ss * 128:(ss + 1) * 128], pcol[:, it:it + 1], None,
                          ALU.is_equal, ALU.is_equal)
                    p.mm(psA[:, :], PTg, xstb[:, dh * 512:(dh + 1) * 512], it == 0, it == 63)
                p.cast(selx[ss][:, dh * 512:(dh + 1) * 512], psA[:, :])

        def attend(xb_tiles, ntok, out_hbm, kfmS, qfmS, vtokS, Ssb, Pb, PTb, osb, xfmS):
            nt = ntok // 128
            for t in range(nt):
                fm_tiles(p, xfmS, xb_tiles[t], psT, ident, t)
            for ot in range(8):
                for dt in range(8):
                    p.mm(psA[:, 0:ntok], wkb[dt][:, ot * 128:(ot + 1) * 128],
                         xfmS[dt][:, 0:ntok], dt == 0, dt == 7)
                p.ts2(kfmS[ot][:, 0:ntok], psA[:, 0:ntok], bks[:, ot:ot + 1], None,
                      ALU.add, ALU.add)
                for dt in range(8):
                    p.mm(psA[:, 0:ntok], wqb[dt][:, ot * 128:(ot + 1) * 128],
                         xfmS[dt][:, 0:ntok], dt == 0, dt == 7)
                p.ts2(qfmS[ot][:, 0:ntok], psA[:, 0:ntok], bqs[:, ot:ot + 1], None,
                      ALU.add, ALU.add)
            for t in range(nt):
                for dh in range(2):
                    for dt in range(8):
                        p.mm(psA[:, :], xfmS[dt][:, t * 128:(t + 1) * 128],
                             wvb[dt][:, dh * 512:(dh + 1) * 512], dt == 0, dt == 7)
                    p.tt(vtokS[t][:, dh * 512:(dh + 1) * 512], psA[:, :],
                         bvs[:, dh * 512:(dh + 1) * 512], ALU.add)
            for qt in range(nt):
                for dt in range(8):
                    p.mm(psB[:, 0:ntok], qfmS[dt][:, qt * 128:(qt + 1) * 128],
                         kfmS[dt][:, 0:ntok], dt == 0, dt == 7)
                p.cast(Ssb[:, 0:ntok], psB[:, 0:ntok])
                p.softmax_unnorm(Pb[:, 0:ntok], Ssb[:, 0:ntok], stats, 0.125)
                for kt in range(nt):
                    p.trans(PTb[:, kt * 128:(kt + 1) * 128], Pb[:, kt * 128:(kt + 1) * 128],
                            psT, ident)
                for dh in range(2):
                    for kt in range(nt):
                        p.mm(psA[:, :], PTb[:, kt * 128:(kt + 1) * 128],
                             vtokS[kt][:, dh * 512:(dh + 1) * 512], kt == 0, kt == nt - 1)
                    p.ts2(osb[:, dh * 512:(dh + 1) * 512], psA[:, :], stats[:, 4:5], None,
                          ALU.mult, ALU.mult)
                p.dma(out_hbm[qt * 128:(qt + 1) * 128, :], osb[:, :])

        kfmS = [p.sb(f"kfmS{i}", [128, 512], BF16) for i in range(8)]
        qfmS = [p.sb(f"qfmS{i}", [128, 512], BF16) for i in range(8)]
        xfmS = [p.sb(f"xfmS{i}", [128, 512], BF16) for i in range(8)]
        vtokS = [p.sb(f"vtokS{i}", [128, D], BF16) for i in range(4)]
        Ssb = p.sb("Ssb", [128, 512])
        Pb = p.sb("Pb", [128, 512], BF16)
        PTb = p.sb("PTb", [128, 512], BF16)
        osb = p.sb("osb", [128, D])
        attend(selx, 512, selout_o, kfmS, qfmS, vtokS, Ssb, Pb, PTb, osb, xfmS)

        wxb = [p.sb(f"wxb{i}", [128, D], BF16) for i in range(2)]
        for j in range(8):
            w = win_base + j
            r0 = 128 * w
            for t in range(2):
                p.dma(xstage[:, :], xnfull[r0 + t * 128:r0 + (t + 1) * 128, :])
                p.cast(wxb[t], xstage)
            attend(wxb, 256, winout_o[j * 256:(j + 1) * 256, :], kfmS, qfmS, vtokS,
                   Ssb, Pb, PTb, osb, xfmS)
        return p
    return build


def build_p4():
    p = Prog()
    x0 = p.din("x0", [S, D])
    xn = p.din("xn", [S, D])
    compb = p.din("compb", [S, D])
    selb = p.din("selb", [S, D])
    winb = p.din("winb", [S, D])
    gates = p.din("gates", [S, 4])
    moe = p.din("moe", [S, D])
    Wo = p.din("Wo", [3 * D, D])
    bo_bc = p.din("bo_bc", [128, D])
    identI = p.din("ident", [128, 128])
    x1_o = p.dout("x1", [S, D])

    ident = p.sb("identb", [128, 128], BF16)
    idf = p.sb("idf", [128, 128], F32)
    p.dma(idf[:, :], identI[:, :])
    p.cast(ident, idf)
    psT = p.ps("psT", [128, 512], BF16)
    psA = p.ps("psA", [128, 512])

    wob = [p.sb(f"wob{i}", [128, D], BF16) for i in range(24)]
    wstage = p.sb("wstage4", [128, D])
    for dt in range(24):
        p.dma(wstage[:, :], Wo[dt * 128:(dt + 1) * 128, :])
        p.cast(wob[dt], wstage)
    bos = p.sb("bos", [128, D])
    p.dma(bos[:, :], bo_bc[:, :])

    bt = p.sb("bt4", [128, D])
    gtile = p.sb("gtile4", [128, 4])
    bbs = [p.sb(f"bbs4{i}", [128, D], BF16) for i in range(3)]
    bfms = p.sb("bfms4", [128, 24 * 128], BF16)
    stile = p.sb("stile4", [128, D])
    xnt = p.sb("xnt4", [128, D])
    x0t = p.sb("x0t4", [128, D])
    scr = p.sb("scr4", [128, D])
    stats = p.sb("stats4", [128, 8])
    gt_ = p.sb("gt4", [128, D])

    for t in range(16):
        p.dma(gtile[:, :], gates[t * 128:(t + 1) * 128, :])
        p.dma(xnt[:, :], xn[t * 128:(t + 1) * 128, :])
        p.dma(x0t[:, :], x0[t * 128:(t + 1) * 128, :])
        for bi, src in enumerate([compb, selb, winb]):
            p.dma(bt[:, :], src[t * 128:(t + 1) * 128, :])
            p.ts2(bbs[bi], bt, gtile[:, bi:bi + 1], None, ALU.mult, ALU.mult)
            for dt in range(8):
                p.trans(bfms[:, (bi * 8 + dt) * 128:(bi * 8 + dt + 1) * 128],
                        bbs[bi][:, dt * 128:(dt + 1) * 128], psT, ident)
        for dh in range(2):
            for k in range(24):
                p.mm(psA[:, :], bfms[:, k * 128:(k + 1) * 128],
                     wob[k][:, dh * 512:(dh + 1) * 512], k == 0, k == 23)
            p.tt(stile[:, dh * 512:(dh + 1) * 512], psA[:, :],
                 bos[:, dh * 512:(dh + 1) * 512], ALU.add)
        p.tss(scr, xnt, 0.5, ALU.mult)
        p.stt(stile, stile, 0.5, scr, ALU.mult, ALU.add)
        p.layernorm(scr, stile, gt_, stats, 1e-6)
        p.dma(gt_[:, :], moe[t * 128:(t + 1) * 128, :])
        p.act(gt_, gt_, AF.Silu)
        p.tt(scr, scr, gt_, ALU.mult)
        p.tt(scr, scr, x0t, ALU.add)
        p.dma(x1_o[t * 128:(t + 1) * 128, :], scr[:, :])
    return p


def build_p5():
    p = Prog()
    x1 = p.din("x1", [S, D])
    WqT = p.din("WqT", [D, D])
    WkT = p.din("WkT", [D, D])
    WvT = p.din("WvT", [D, D])
    bqc = p.din("bqc", [D, 1])
    bkc = p.din("bkc", [D, 1])
    bv_bc = p.din("bv_bc", [128, D])
    identI = p.din("ident", [128, 128])
    xn1_o = p.dout("xn1", [S, D])
    qfm_o = p.dout("qfm", [D, S])
    kfm_o = p.dout("kfm", [D, S])
    vtok_o = p.dout("vtok", [S, D])

    ident = p.sb("identb", [128, 128], BF16)
    idf = p.sb("idf", [128, 128], F32)
    p.dma(idf[:, :], identI[:, :])
    p.cast(ident, idf)
    psT = p.ps("psT", [128, 512], BF16)
    psA = p.ps("psA", [128, 512])

    xt = p.sb("xt5", [128, D])
    scr = p.sb("scr5", [128, D])
    stats = p.sb("stats5", [128, 8])
    xnt = p.sb("xnt5", [128, D])
    xnb = p.sb("xnb5", [128, D], BF16)
    xfm = [p.sb(f"xfm5{i}", [128, S], BF16) for i in range(8)]
    for t in range(16):
        p.dma(xt[:, :], x1[t * 128:(t + 1) * 128, :])
        p.layernorm(xnt, xt, scr, stats, 1e-5)
        p.dma(xn1_o[t * 128:(t + 1) * 128, :], xnt[:, :])
        p.cast(xnb, xnt)
        fm_tiles(p, xfm, xnb, psT, ident, t)

    wqb = [p.sb(f"wq5{i}", [128, D], BF16) for i in range(8)]
    wkb = [p.sb(f"wk5{i}", [128, D], BF16) for i in range(8)]
    wvb = [p.sb(f"wv5{i}", [128, D], BF16) for i in range(8)]
    wstage = p.sb("wstage5", [128, D])
    for dt in range(8):
        p.dma(wstage[:, :], WqT[dt * 128:(dt + 1) * 128, :])
        p.cast(wqb[dt], wstage)
        p.dma(wstage[:, :], WkT[dt * 128:(dt + 1) * 128, :])
        p.cast(wkb[dt], wstage)
        p.dma(wstage[:, :], WvT[dt * 128:(dt + 1) * 128, :])
        p.cast(wvb[dt], wstage)
    bqs = p.sb("bqs5", [128, 8])
    bks = p.sb("bks5", [128, 8])
    for dt in range(8):
        p.dma(bqs[:, dt:dt + 1], bqc[dt * 128:(dt + 1) * 128, :])
        p.dma(bks[:, dt:dt + 1], bkc[dt * 128:(dt + 1) * 128, :])
    bvs = p.sb("bvs5", [128, D])
    p.dma(bvs[:, :], bv_bc[:, :])
    osb = p.sb("ot5", [128, 512])

    for ot in range(8):
        for ch in range(4):
            for dt in range(8):
                p.mm(psA[:, :], wqb[dt][:, ot * 128:(ot + 1) * 128],
                     xfm[dt][:, ch * 512:(ch + 1) * 512], dt == 0, dt == 7)
            p.ts2(osb, psA[:, :], bqs[:, ot:ot + 1], None, ALU.add, ALU.add)
            p.dma(qfm_o[ot * 128:(ot + 1) * 128, ch * 512:(ch + 1) * 512], osb[:, :])
            for dt in range(8):
                p.mm(psA[:, :], wkb[dt][:, ot * 128:(ot + 1) * 128],
                     xfm[dt][:, ch * 512:(ch + 1) * 512], dt == 0, dt == 7)
            p.ts2(osb, psA[:, :], bks[:, ot:ot + 1], None, ALU.add, ALU.add)
            p.dma(kfm_o[ot * 128:(ot + 1) * 128, ch * 512:(ch + 1) * 512], osb[:, :])
    for t in range(16):
        for dh in range(2):
            for dt in range(8):
                p.mm(psA[:, :], xfm[dt][:, t * 128:(t + 1) * 128],
                     wvb[dt][:, dh * 512:(dh + 1) * 512], dt == 0, dt == 7)
            p.tt(osb, psA[:, :], bvs[:, dh * 512:(dh + 1) * 512], ALU.add)
            p.dma(vtok_o[t * 128:(t + 1) * 128, dh * 512:(dh + 1) * 512], osb[:, :])
    return p


def build_p6():
    p = Prog()
    qf = p.din("qf", [4 * 64, LSEQ])
    kf = p.din("kf", [4 * 64, LSEQ])
    va = p.din("va", [4 * LSEQ, 65])
    lam_bc = p.din("lam_bc", [128, 1])
    iq_bc = p.din("iq_bc", [128, 512])
    ik = p.din("ik", [128, 1])
    ofm_o = p.dout("ofm", [4 * 64, LSEQ])

    psS = p.ps("psS6", [128, 512])
    psO = p.ps("psO6", [128, 512])
    psB6 = p.ps("psB6", [64, 512])

    lam = p.sb("lam6", [128, 1])
    p.dma(lam[:, :], lam_bc[:, :])
    iqb = p.sb("iqb6", [128, 512])
    p.dma(iqb[:, :], iq_bc[:, :])
    ikb = p.sb("ikb6", [128, 1])
    p.dma(ikb[:, :], ik[:, :])

    stage = p.sb("stage6", [128, 2048])
    qb = p.sb("qb6", [64, LSEQ], BF16)
    kb = p.sb("kb6", [64, LSEQ], BF16)
    vb = p.sb("vb6", [128, 64 * 65], BF16)
    ko = p.sb("ko6", [128, 1])
    dq = p.sb("dq6", [128, 512])
    u1 = p.sb("u16", [128, 512])
    e1 = p.sb("e16", [128, 512])
    e2 = p.sb("e26", [128, 512])
    masks = [p.sb(f"mask6{i}", [128, 512]) for i in range(6)]
    expst = p.sb("expst6", [128, 64 * 512], BF16)
    tmpf = p.sb("tmpf6", [128, 512])
    rb32 = p.sb("rb326", [1, 512])
    rb = p.sb("rb6", [1, 512], BF16)
    ones1 = p.sb("ones16", [1, 64], BF16)
    p.memset(ones1, 1.0)
    outsb = p.sb("outsb6", [64, 512])
    rbc = p.sb("rbc6", [64, 512])

    SC = 0.125
    for u in range(4):
        for ch in range(4):
            p.dma(stage[0:64, :], qf[u * 64:(u + 1) * 64, ch * 2048:(ch + 1) * 2048])
            p.cast(qb[:, ch * 2048:(ch + 1) * 2048], stage[0:64, :])
            p.dma(stage[0:64, :], kf[u * 64:(u + 1) * 64, ch * 2048:(ch + 1) * 2048])
            p.cast(kb[:, ch * 2048:(ch + 1) * 2048], stage[0:64, :])
        for kt in range(64):
            p.dma(stage[:, 0:65], va[u * LSEQ + kt * 128:u * LSEQ + (kt + 1) * 128, :])
            p.cast(vb[:, kt * 65:(kt + 1) * 65], stage[:, 0:65])
        for qc in range(16):
            kt_lo = max(0, (qc * 512 - 63) // 128)
            kt_hi = min(64, (qc * 512 + 574) // 128 + 1)
            band = list(range(kt_lo, kt_hi))
            for bi, kt in enumerate(band):
                m = masks[bi]
                p.tss(ko, ikb, float(kt * 128 - qc * 512), ALU.add)
                p.ts2(dq, iqb, ko[:, 0:1], None, ALU.subtract, ALU.subtract)
                p.act(dq, dq, AF.Abs)
                p.tss(u1, dq, 1.0, ALU.subtract)
                p.act(u1, u1, AF.Square)
                p.act(e1, u1, AF.Exp, scale=-2.0)
                p.tss(u1, dq, 30.0, ALU.subtract)
                p.act(u1, u1, AF.Square)
                p.act(e2, u1, AF.Exp, scale=-0.125)
                p.smul(e2, e2, 0.4)
                p.stt(m, e1, 0.8, e2, ALU.mult, ALU.add)
                p.ts2(m, m, lam[:, 0:1], None, ALU.mult, ALU.mult)
            for kt in range(64):
                p.mm(psS[:, :], kb[0:64, kt * 128:(kt + 1) * 128],
                     qb[0:64, qc * 512:(qc + 1) * 512], True, True)
                dst = expst[:, kt * 512:(kt + 1) * 512]
                if kt in band:
                    m = masks[band.index(kt)]
                    p.tss(tmpf, psS[:, :], SC, ALU.mult)
                    p.tt(tmpf, tmpf, m, ALU.add)
                    p.act(dst, tmpf, AF.Exp)
                else:
                    p.act(dst, psS[:, :], AF.Exp, scale=SC)
            for kt in range(64):
                p.mm(psO[0:65, :], vb[:, kt * 65:(kt + 1) * 65],
                     expst[:, kt * 512:(kt + 1) * 512], kt == 0, kt == 63)
            p.recip(rb32[0:1, :], psO[64:65, :])
            p.cast(rb[0:1, :], rb32[0:1, :])
            p.mm(psB6[:, :], ones1, rb[0:1, :], True, True)
            p.cast(rbc, psB6[:, :])
            p.tt(outsb, psO[0:64, :], rbc, ALU.mult)
            p.dma(ofm_o[u * 64:(u + 1) * 64, qc * 512:(qc + 1) * 512], outsb[:, :])
    return p


RW = 2432


def build_p7():
    p = Prog()
    xin = p.din("xin", [RW, D])
    Wqkv = p.din("Wqkv", [D, 3072])
    Wo = p.din("Wo", [D, D])
    qs_bc = p.din("qs_bc", [128, D])
    ks_bc = p.din("ks_bc", [128, D])
    cq = p.din("cq", [RW, D])
    sq = p.din("sq", [RW, D])
    ck = p.din("ck", [RW, D])
    sk = p.din("sk", [RW, D])
    maskc = p.din("maskc", [8 * 2 * 128, 512])
    identI = p.din("ident", [128, 128])
    yo = p.dout("yout", [S, D])

    ident = p.sb("identb", [128, 128], BF16)
    idf = p.sb("idf", [128, 128], F32)
    p.dma(idf[:, :], identI[:, :])
    p.cast(ident, idf)
    psT = p.ps("psT7", [128, 512], BF16)
    psA = p.ps("psA7", [128, 512])
    psB = p.ps("psB7", [128, 512])

    wqkvb = [p.sb(f"wqkv7{i}", [128, 3072], BF16) for i in range(8)]
    wob = [p.sb(f"wo7{i}", [128, D], BF16) for i in range(8)]
    wstage = p.sb("wstage7", [128, 3072])
    for dt in range(8):
        p.dma(wstage[:, :], Wqkv[dt * 128:(dt + 1) * 128, :])
        p.cast(wqkvb[dt], wstage)
        p.dma(wstage[:, 0:D], Wo[dt * 128:(dt + 1) * 128, :])
        p.cast(wob[dt], wstage[:, 0:D])
    qsb = p.sb("qsb7", [128, D])
    ksb = p.sb("ksb7", [128, D])
    p.dma(qsb[:, :], qs_bc[:, :])
    p.dma(ksb[:, :], ks_bc[:, :])

    xs = p.sb("xs7", [128, D])
    xsb = p.sb("xsb7", [128, D], BF16)
    xfms = p.sb("xfms7", [128, 8 * 128], BF16)
    qkv = p.sb("qkv7", [128, 3072])
    rot = p.sb("rot7", [128, 2048])
    tab = p.sb("tab7", [128, 2048])
    nrm = p.sb("nrm7", [128, 32])
    t1 = p.sb("t17", [128, 2048])
    qcur = [p.sb(f"qcur{i}", [128, D], BF16) for i in range(2)]
    kblk = [p.sb(f"kblk{i}", [128, D], BF16) for i in range(4)]
    vblk = [p.sb(f"vblk{i}", [128, D], BF16) for i in range(4)]
    qfmh = p.sb("qfmh7", [64, 256], BF16)
    kfmh = p.sb("kfmh7", [64, 512], BF16)
    Ss = p.sb("Ss7", [128, 512])
    Pbf = p.sb("Pbf7", [128, 512], BF16)
    PTbf = p.sb("PTbf7", [128, 512], BF16)
    stats = p.sb("stats7", [128, 8])
    msk = [p.sb(f"msk7{i}", [128, 512]) for i in range(2)]
    otile = [p.sb(f"otile7{i}", [128, D], BF16) for i in range(2)]
    ofm = p.sb("ofm7", [128, 128], BF16)
    res = p.sb("res7", [128, D])
    outf = p.sb("outf7", [128, 512])
    ov = p.sb("ov7", [128, 64])

    def qkv_tile(r0, kslot, vslot, qslot):
        p.dma(xs[:, 0:512], xin[r0:r0 + 128, 0:512])
        p.dma(xs[:, 512:D], xin[r0 - 1:r0 + 127, 512:D])
        p.cast(xsb, xs)
        for dt in range(8):
            p.trans(xfms[:, dt * 128:(dt + 1) * 128], xsb[:, dt * 128:(dt + 1) * 128],
                    psT, ident)
        for ch in range(6):
            for dt in range(8):
                p.mm(psA[:, :], xfms[:, dt * 128:(dt + 1) * 128],
                     wqkvb[dt][:, ch * 512:(ch + 1) * 512], dt == 0, dt == 7)
            p.cast(qkv[:, ch * 512:(ch + 1) * 512], psA[:, :])
        p.act(t1, qkv[:, 0:2048], AF.Square)
        p.v(lambda e, o=A(nrm), i=A(t1): e.reduce_sum(
            out=o, in_=i.rearrange("p (h f) -> p h f", h=32), axis=AX.X))
        p.act(nrm, nrm, AF.Sqrt)
        p.tss(nrm, nrm, 1e-12, ALU.max)
        p.recip(nrm, nrm)
        for h in range(32):
            p.ts2(qkv[:, h * 64:(h + 1) * 64], qkv[:, h * 64:(h + 1) * 64],
                  nrm[:, h:h + 1], None, ALU.mult, ALU.mult)
        p.tt(qkv[:, 0:D], qkv[:, 0:D], qsb, ALU.mult)
        p.tt(qkv[:, D:2048], qkv[:, D:2048], ksb, ALU.mult)
        for h in range(32):
            base = h * 64
            p.smul(rot[:, base:base + 32], qkv[:, base + 32:base + 64], -1.0)
            p.smul(rot[:, base + 32:base + 64], qkv[:, base:base + 32], 1.0)
        p.dma(tab[:, 0:D], cq[r0:r0 + 128, :])
        p.dma(tab[:, D:2048], ck[r0:r0 + 128, :])
        p.tt(t1, qkv[:, 0:2048], tab, ALU.mult)
        p.dma(tab[:, 0:D], sq[r0:r0 + 128, :])
        p.dma(tab[:, D:2048], sk[r0:r0 + 128, :])
        p.tt(rot, rot, tab, ALU.mult)
        p.tt(t1, t1, rot, ALU.add)
        p.cast(qslot, t1[:, 0:D])
        p.cast(kslot, t1[:, D:2048])
        p.cast(vslot, qkv[:, 2048:3072])

    for blk in range(8):
        r0 = 384 + blk * 256
        if blk == 0:
            qkv_tile(r0 - 256, kblk[0], vblk[0], qcur[0])
            qkv_tile(r0 - 128, kblk[1], vblk[1], qcur[0])
        else:
            for i in range(2):
                p.cast(kblk[i], kblk[2 + i])
                p.cast(vblk[i], vblk[2 + i])
        qkv_tile(r0, kblk[2], vblk[2], qcur[0])
        qkv_tile(r0 + 128, kblk[3], vblk[3], qcur[1])
        for qt in range(2):
            p.dma(msk[qt][:, :], maskc[(blk * 2 + qt) * 128:(blk * 2 + qt + 1) * 128, :])
        for h in range(16):
            for i in range(2):
                p.trans(qfmh[:, i * 128:(i + 1) * 128], qcur[i][:, h * 64:(h + 1) * 64],
                        psT, ident, p=128, f=64)
            for i in range(4):
                p.trans(kfmh[:, i * 128:(i + 1) * 128], kblk[i][:, h * 64:(h + 1) * 64],
                        psT, ident, p=128, f=64)
            for qt in range(2):
                p.mm(psA[:, :], qfmh[0:64, qt * 128:(qt + 1) * 128], kfmh[0:64, :], True, True)
                p.tt(Ss, psA[:, :], msk[qt], ALU.add)
                p.softmax_unnorm(Pbf, Ss, stats, 8.0)
                for i in range(4):
                    p.trans(PTbf[:, i * 128:(i + 1) * 128], Pbf[:, i * 128:(i + 1) * 128],
                            psT, ident)
                for i in range(4):
                    p.mm(psB[:, 0:64], PTbf[:, i * 128:(i + 1) * 128],
                         vblk[i][:, h * 64:(h + 1) * 64], i == 0, i == 3)
                p.ts2(ov, psB[:, 0:64], stats[:, 4:5], None, ALU.mult, ALU.mult)
                p.cast(otile[qt][:, h * 64:(h + 1) * 64], ov)
        for qt in range(2):
            r = r0 + qt * 128
            p.dma(res[:, :], xin[r:r + 128, :])
            for dt in range(8):
                p.trans(xfms[:, dt * 128:(dt + 1) * 128], otile[qt][:, dt * 128:(dt + 1) * 128],
                        psT, ident)
            for dh in range(2):
                for dt in range(8):
                    p.mm(psA[:, :], xfms[:, dt * 128:(dt + 1) * 128],
                         wob[dt][:, dh * 512:(dh + 1) * 512], dt == 0, dt == 7)
                p.tt(outf, psA[:, :], res[:, dh * 512:(dh + 1) * 512], ALU.add)
                p.dma(yo[r - 384:r - 384 + 128, dh * 512:(dh + 1) * 512], outf[:, :])
    return p


RW8 = 2176


def build_p8():
    p = Prog()
    xin = p.din("xin", [RW8, D])
    lng_bc = p.din("lng_bc", [128, D])
    lnb_bc = p.din("lnb_bc", [128, D])
    W1 = p.din("W1", [D, 5460])
    b1_bc = p.din("b1_bc", [128, 5460])
    W2p = p.din("W2p", [2816, D])
    b2_bc = p.din("b2_bc", [128, D])
    identI = p.din("ident", [128, 128])
    yo = p.dout("yout", [S, D])

    ident = p.sb("identb", [128, 128], BF16)
    idf = p.sb("idf", [128, 128], F32)
    p.dma(idf[:, :], identI[:, :])
    p.cast(ident, idf)
    psT = p.ps("psT8", [128, 512], BF16)
    psA = p.ps("psA8", [128, 512])

    w1b = [p.sb(f"w18{i}", [128, 5460], BF16) for i in range(8)]
    w2b = [p.sb(f"w28{i}", [128, D], BF16) for i in range(22)]
    wstage = p.sb("wstage8", [128, 1365])
    for dt in range(8):
        for q4 in range(4):
            p.dma(wstage[:, :], W1[dt * 128:(dt + 1) * 128, q4 * 1365:(q4 + 1) * 1365])
            p.cast(w1b[dt][:, q4 * 1365:(q4 + 1) * 1365], wstage)
    for it in range(22):
        p.dma(wstage[:, 0:1024], W2p[it * 128:(it + 1) * 128, :])
        p.cast(w2b[it], wstage[:, 0:1024])
    lng = p.sb("lng8", [128, D])
    lnb = p.sb("lnb8", [128, D])
    b1s = p.sb("b1s8", [128, 5460], BF16)
    b1f = p.sb("b1f8", [128, 1365])
    b2s = p.sb("b2s8", [128, D])
    p.dma(lng[:, :], lng_bc[:, :])
    p.dma(lnb[:, :], lnb_bc[:, :])
    for q4 in range(4):
        p.dma(b1f[:, :], b1_bc[:, q4 * 1365:(q4 + 1) * 1365])
        p.cast(b1s[:, q4 * 1365:(q4 + 1) * 1365], b1f)
    p.dma(b2s[:, :], b2_bc[:, :])

    xs = p.sb("xs8", [128, D])
    xnb = p.sb("xnb8", [128, D], BF16)
    res = p.sb("res8", [128, D])
    stats = p.sb("stats8", [128, 8])
    xfms = p.sb("xfms8", [128, 8 * 128], BF16)
    hsb = p.sb("hsb8", [128, 5460], BF16)
    gel = p.sb("gel8", [128, 2730], BF16)
    prod = p.sb("prod8", [128, 2816], BF16)
    hfms = p.sb("hfms8", [128, 22 * 128], BF16)
    outf = p.sb("outf8", [128, 512])

    ch_sizes = [512] * 10 + [340]
    for t in range(16):
        r0 = 128 + t * 128
        p.dma(xs[:, 0:512], xin[r0:r0 + 128, 0:512])
        p.dma(xs[:, 512:D], xin[r0 - 1:r0 + 127, 512:D])
        p.layernorm(xs, xs, res, stats, 1e-5)
        p.tt(xs, xs, lng, ALU.mult)
        p.tt(xs, xs, lnb, ALU.add)
        p.cast(xnb, xs)
        for dt in range(8):
            p.trans(xfms[:, dt * 128:(dt + 1) * 128], xnb[:, dt * 128:(dt + 1) * 128],
                    psT, ident)
        off = 0
        for w in ch_sizes:
            for dt in range(8):
                p.mm(psA[:, 0:w], xfms[:, dt * 128:(dt + 1) * 128], w1b[dt][:, off:off + w],
                     dt == 0, dt == 7)
            p.tt(hsb[:, off:off + w], psA[:, 0:w], b1s[:, off:off + w], ALU.add)
            off += w
        p.act(gel, hsb[:, 2730:5460], AF.Gelu)
        p.memset(prod[:, 2730:2816], 0.0)
        p.tt(prod[:, 0:2730], hsb[:, 0:2730], gel, ALU.mult)
        for it in range(22):
            p.trans(hfms[:, it * 128:(it + 1) * 128], prod[:, it * 128:(it + 1) * 128],
                    psT, ident)
        p.dma(res[:, :], xin[r0:r0 + 128, :])
        for dh in range(2):
            for it in range(22):
                p.mm(psA[:, :], hfms[:, it * 128:(it + 1) * 128],
                     w2b[it][:, dh * 512:(dh + 1) * 512], it == 0, it == 21)
            p.tt(outf, psA[:, :], b2s[:, dh * 512:(dh + 1) * 512], ALU.add)
            p.tt(outf, outf, res[:, dh * 512:(dh + 1) * 512], ALU.add)
            p.dma(yo[t * 128:(t + 1) * 128, dh * 512:(dh + 1) * 512], outf[:, :])
    return p


def build_p9():
    p = Prog()
    ofm = p.din("ofm", [D, S])
    WoT = p.din("WoT", [D, D])
    bo_bc = p.din("bo_bc", [128, D])
    yfin = p.din("yfin", [S, D])
    x1 = p.din("x1", [S, D])
    out_o = p.dout("out", [S, D])

    psA = p.ps("psA9", [128, 512])
    wob = [p.sb(f"wo9{i}", [128, D], BF16) for i in range(8)]
    wstage = p.sb("wstage9", [128, D])
    for dt in range(8):
        p.dma(wstage[:, :], WoT[dt * 128:(dt + 1) * 128, :])
        p.cast(wob[dt], wstage)
    bos = p.sb("bos9", [128, D])
    p.dma(bos[:, :], bo_bc[:, :])
    ofb = [p.sb(f"of9{i}", [128, S], BF16) for i in range(8)]
    stage = p.sb("stage9", [128, S])
    for dt in range(8):
        p.dma(stage[:, :], ofm[dt * 128:(dt + 1) * 128, :])
        p.cast(ofb[dt], stage)
    yt = p.sb("yt9", [128, D])
    x1t = p.sb("x1t9", [128, D])
    st = p.sb("st9", [128, D])
    for t in range(16):
        p.dma(yt[:, :], yfin[t * 128:(t + 1) * 128, :])
        p.dma(x1t[:, :], x1[t * 128:(t + 1) * 128, :])
        for dh in range(2):
            for dt in range(8):
                p.mm(psA[:, :], ofb[dt][:, t * 128:(t + 1) * 128],
                     wob[dt][:, dh * 512:(dh + 1) * 512], dt == 0, dt == 7)
            p.tt(st[:, dh * 512:(dh + 1) * 512], psA[:, :], bos[:, dh * 512:(dh + 1) * 512],
                 ALU.add)
        p.act(yt, yt, AF.Silu)
        p.tt(st, st, yt, ALU.mult)
        p.tt(st, st, x1t, ALU.add)
        p.dma(out_o[t * 128:(t + 1) * 128, :], st[:, :])
    return p


def _run(build_fn, in_maps, outs):
    p = build_fn()
    nc = p.fin()
    maps = [{k: np.ascontiguousarray(v, dtype=np.float32) for k, v in m.items()}
            for m in in_maps]
    res = run_bass_kernel_spmd(nc, maps, core_ids=list(range(NCORES)))
    return [[np.asarray(res.results[c][o]) for o in outs] for c in range(NCORES)]


def kernel(**inp):
    f32 = np.float32
    x0 = np.ascontiguousarray(np.asarray(inp["L0_attn_hidden_states"], f32)).reshape(16384, D)
    ident = np.eye(128, dtype=f32)
    ltri = np.tril(np.ones((128, 128), f32), -1)
    iota_bc = np.broadcast_to(np.arange(512, dtype=f32), (128, 512)).copy()

    def bc(v, n=128):
        v = np.asarray(v, f32).reshape(-1)
        return np.broadcast_to(v.reshape(1, -1), (n, v.size)).copy()

    in1 = [dict(x=x0[c * S:(c + 1) * S], Wg=inp["L0_attn_Wg"], bg_bc=bc(inp["L0_attn_bg"]),
                Ws=inp["L0_attn_Ws"], bs_bc=np.broadcast_to(
                    np.asarray(inp["L0_attn_bs"], f32).reshape(1, 1), (128, 1)).copy(),
                Wc=inp["L0_attn_Wc"], bc_bc=bc(inp["L0_attn_bc"]), ident=ident)
           for c in range(8)]
    r1 = _run(build_p1, in1, ["xn", "gates", "scores", "comp"])
    xn = np.concatenate([r1[c][0] for c in range(8)], 0)
    gates = np.concatenate([r1[c][1] for c in range(8)], 0)
    scores = np.concatenate([r1[c][2] for c in range(8)], 0)
    comp = np.concatenate([r1[c][3] for c in range(8)], 0)

    w1T = np.concatenate([np.asarray(inp["L0_ffn_w1"], f32).transpose(0, 2, 1),
                          np.asarray(inp["L0_ffn_ws1"], f32).T[None]], 0)
    w3T = np.concatenate([np.asarray(inp["L0_ffn_w3"], f32).transpose(0, 2, 1),
                          np.asarray(inp["L0_ffn_ws3"], f32).T[None]], 0)
    w2T = np.concatenate([np.asarray(inp["L0_ffn_w2"], f32).transpose(0, 2, 1),
                          np.asarray(inp["L0_ffn_ws2"], f32).T[None]], 0)
    in2 = [dict(xn=xn[c * S:(c + 1) * S], gwT=np.asarray(inp["L0_ffn_gate_w"], f32).T,
                gb_bc=bc(inp["L0_ffn_gate_bias"]), w1T=w1T, w3T=w3T, w2T=w2T, ident=ident)
           for c in range(8)]
    r2 = _run(build_p2, in2, ["moe"])
    moe = np.concatenate([r2[c][0] for c in range(8)], 0)

    in3k = [dict(comp=comp[c * 512:(c + 1) * 512], Wq=inp["L0_attn_Wq"],
                 bqc=np.asarray(inp["L0_attn_bq"], f32).reshape(D, 1), Wk=inp["L0_attn_Wk"],
                 bkc=np.asarray(inp["L0_attn_bk"], f32).reshape(D, 1), Wv=inp["L0_attn_Wv"],
                 bv_bc=bc(inp["L0_attn_bv"]), ident=ident) for c in range(8)]
    r3k = _run(build_p3k, in3k, ["qfm", "kfm", "vtok"])
    qfmC = [np.concatenate([r3k[b * 4 + g][0] for g in range(4)], 1) for b in range(2)]
    kfmC = [np.concatenate([r3k[b * 4 + g][1] for g in range(4)], 1) for b in range(2)]
    vtokC = [np.concatenate([r3k[b * 4 + g][2] for g in range(4)], 0) for b in range(2)]

    in3a = [dict(qfm=qfmC[c // 4][:, (c % 4) * 512:(c % 4 + 1) * 512], kfm=kfmC[c // 4],
                 vtok=vtokC[c // 4], ident=ident) for c in range(8)]
    r3a = _run(build_p3a, in3a, ["compout"])
    compout = np.stack([np.concatenate([r3a[b * 4 + g][0] for g in range(4)], 0)
                        for b in range(2)], 0)

    selout = np.zeros((2, 512, D), f32)
    winout = np.zeros((2, LSEQ, D), f32)
    for grp in range(4):
        in3b = []
        for c in range(8):
            b = c // 4
            in3b.append(dict(
                xnfull=xn[b * LSEQ:(b + 1) * LSEQ], scores=scores[b * LSEQ:(b + 1) * LSEQ],
                Wq=inp["L0_attn_Wq"], bqc=np.asarray(inp["L0_attn_bq"], f32).reshape(D, 1),
                Wk=inp["L0_attn_Wk"], bkc=np.asarray(inp["L0_attn_bk"], f32).reshape(D, 1),
                Wv=inp["L0_attn_Wv"], bv_bc=bc(inp["L0_attn_bv"]), ident=ident,
                iota_bc=iota_bc, ltri=ltri))
        r3b = _run(make_p3b(grp * 8), in3b, ["selout", "winout"])
        for b in range(2):
            c = b * 4 + grp
            if grp == 0:
                selout[b] = r3b[c][0]
            winout[b, grp * 2048:(grp + 1) * 2048] = r3b[c][1]

    in4 = []
    for c in range(8):
        b = c // 4
        t0 = (c % 4) * S
        cslice = compout[b] if t0 == 0 else np.zeros((S, D), f32)
        sslice = np.zeros((S, D), f32)
        if t0 == 0:
            sslice[0:512] = selout[b]
        in4.append(dict(x0=x0[c * S:(c + 1) * S], xn=xn[c * S:(c + 1) * S], compb=cslice,
                        selb=sslice, winb=winout[b, t0:t0 + S], gates=gates[c * S:(c + 1) * S],
                        moe=moe[c * S:(c + 1) * S], Wo=inp["L0_attn_Wo"],
                        bo_bc=bc(inp["L0_attn_bo"]), ident=ident))
    r4 = _run(build_p4, in4, ["x1"])
    x1 = np.concatenate([r4[c][0] for c in range(8)], 0)

    in5 = [dict(x1=x1[c * S:(c + 1) * S], WqT=np.asarray(inp["L1_attn_Wq"], f32).T,
                WkT=np.asarray(inp["L1_attn_Wk"], f32).T, WvT=np.asarray(inp["L1_attn_Wv"], f32).T,
                bqc=np.asarray(inp["L1_attn_bq"], f32).reshape(D, 1),
                bkc=np.asarray(inp["L1_attn_bk"], f32).reshape(D, 1),
                bv_bc=bc(inp["L1_attn_bv"]), ident=ident) for c in range(8)]
    r5 = _run(build_p5, in5, ["xn1", "qfm", "kfm", "vtok"])
    xn1 = np.concatenate([r5[c][0] for c in range(8)], 0)
    qfm = np.concatenate([r5[c][1] for c in range(8)], 1)
    kfm = np.concatenate([r5[c][2] for c in range(8)], 1)
    vtok = np.concatenate([r5[c][3] for c in range(8)], 0)

    lam = np.asarray(inp["L1_attn_prior_lambda"], f32).reshape(1, 1)
    in6 = []
    for c in range(8):
        qs, ks, vs = [], [], []
        for u in range(4):
            gu = c * 4 + u
            b, h = gu // 16, gu % 16
            qs.append(qfm[h * 64:(h + 1) * 64, b * LSEQ:(b + 1) * LSEQ])
            ks.append(kfm[h * 64:(h + 1) * 64, b * LSEQ:(b + 1) * LSEQ])
            vh = vtok[b * LSEQ:(b + 1) * LSEQ, h * 64:(h + 1) * 64]
            vs.append(np.concatenate([vh, np.ones((LSEQ, 1), f32)], 1))
        in6.append(dict(qf=np.concatenate(qs, 0), kf=np.concatenate(ks, 0),
                        va=np.concatenate(vs, 0),
                        lam_bc=np.broadcast_to(lam, (128, 1)).copy(), iq_bc=iota_bc,
                        ik=np.arange(128, dtype=f32).reshape(128, 1)))
    r6 = _run(build_p6, in6, ["ofm"])
    ofm = np.zeros((2, D, LSEQ), f32)
    for c in range(8):
        for u in range(4):
            gu = c * 4 + u
            b, h = gu // 16, gu % 16
            ofm[b, h * 64:(h + 1) * 64] = r6[c][0][u * 64:(u + 1) * 64]

    DIM_HEAD, WINDOW = 64, 256
    n = LSEQ
    half = np.arange(0, DIM_HEAD, 2, dtype=np.float64)
    inv_freq = 1.0 / 10000.0 ** (half / DIM_HEAD)
    tt_ = np.arange(n, dtype=np.float64)
    freqs = np.concatenate([tt_[:, None] * inv_freq[None, :]] * 2, 1)
    base_scale = (half + 0.4 * DIM_HEAD) / (1.4 * DIM_HEAD)
    power = (tt_ - n // 2) / (WINDOW // 2)
    scale = np.concatenate([base_scale[None, :] ** power[:, None]] * 2, 1)
    cqT = np.tile((np.cos(freqs) * scale).astype(f32), (1, 16))
    sqT = np.tile((np.sin(freqs) * scale).astype(f32), (1, 16))
    ckT = np.tile((np.cos(freqs) / scale).astype(f32), (1, 16))
    skT = np.tile((np.sin(freqs) / scale).astype(f32), (1, 16))

    NEG = -np.finfo(np.float32).max
    i_loc = np.arange(256)[:, None]
    j_loc = np.arange(512)[None, :]
    m_norm = np.where((j_loc >= i_loc) & (j_loc <= i_loc + 256), 0.0, NEG).astype(f32)
    m_first = np.where((j_loc >= 256) & (j_loc - 256 <= i_loc) & (i_loc <= j_loc - 256 + 256)
                       & (i_loc >= j_loc - 256), 0.0, NEG).astype(f32)
    m_norm = np.where((i_loc >= j_loc - 256) & (i_loc <= j_loc), 0.0, NEG).astype(f32)
    m_first = np.where((j_loc >= 256) & (i_loc >= j_loc - 256) & (i_loc <= j_loc - 256 + 256)
                       & (i_loc <= j_loc), 0.0, NEG).astype(f32)
    m_first = np.where((j_loc >= 256) & (i_loc >= j_loc - 256) & (i_loc <= j_loc), 0.0,
                       NEG).astype(f32)

    def mask_for_core(c):
        out = []
        for blk in range(8):
            gblk = (c % 4) * 8 + blk
            m = m_first if gblk == 0 else m_norm
            out.append(m.reshape(2 * 128, 512))
        return np.concatenate(out, 0)

    maskcs = [mask_for_core(c) for c in range(8)]

    y = xn1.copy()
    for li in range(6):
        in7 = []
        for c in range(8):
            b = c // 4
            t0 = (c % 4) * S
            lo = max(0, t0 - 384)
            pad = 384 - (t0 - lo)
            xin = np.zeros((RW, D), f32)
            xin[pad:] = y[b * LSEQ + lo: b * LSEQ + t0 + S]
            tabs = []
            for tab in (cqT, sqT, ckT, skT):
                tp = np.zeros((RW, D), f32)
                tp[pad:] = tab[lo:t0 + S]
                tabs.append(tp)
            in7.append(dict(xin=xin, Wqkv=np.asarray(inp["L1_ffn_Wqkv"][li], f32),
                            Wo=np.asarray(inp["L1_ffn_Wo"][li], f32),
                            qs_bc=bc(np.tile(np.asarray(inp["L1_ffn_q_scale"][li], f32), 16)),
                            ks_bc=bc(np.tile(np.asarray(inp["L1_ffn_k_scale"][li], f32), 16)),
                            cq=tabs[0], sq=tabs[1], ck=tabs[2], sk=tabs[3],
                            maskc=maskcs[c], ident=ident))
        r7 = _run(build_p7, in7, ["yout"])
        y = np.concatenate([r7[c][0] for c in range(8)], 0)

        W2p = np.concatenate([np.asarray(inp["L1_ffn_W2"][li], f32),
                              np.zeros((2816 - 2730, D), f32)], 0)
        in8 = []
        for c in range(8):
            b = c // 4
            t0 = (c % 4) * S
            lo = max(0, t0 - 128)
            pad = 128 - (t0 - lo)
            xin = np.zeros((RW8, D), f32)
            xin[pad:] = y[b * LSEQ + lo: b * LSEQ + t0 + S]
            in8.append(dict(xin=xin, lng_bc=bc(inp["L1_ffn_ln_g"][li]),
                            lnb_bc=bc(inp["L1_ffn_ln_b"][li]),
                            W1=np.asarray(inp["L1_ffn_W1"][li], f32),
                            b1_bc=bc(inp["L1_ffn_b1"][li]), W2p=W2p,
                            b2_bc=bc(inp["L1_ffn_b2"][li]), ident=ident))
        r8 = _run(build_p8, in8, ["yout"])
        y = np.concatenate([r8[c][0] for c in range(8)], 0)

    in9 = []
    for c in range(8):
        b = c // 4
        t0 = (c % 4) * S
        in9.append(dict(ofm=ofm[b][:, t0:t0 + S], WoT=np.asarray(inp["L1_attn_Wo"], f32).T,
                        bo_bc=bc(inp["L1_attn_bo"]), yfin=y[c * S:(c + 1) * S],
                        x1=x1[c * S:(c + 1) * S]))
    r9 = _run(build_p9, in9, ["out"])
    out = np.concatenate([r9[c][0] for c in range(8)], 0)
    return out.reshape(2, 8192, 1024).astype(np.float32)
